# version 02
# baseline (speedup 1.0000x reference)
import numpy as np
from contextlib import ExitStack

import concourse.bass as bass
import concourse.mybir as mybir
from concourse.bass_utils import run_bass_kernel_spmd
from concourse.ap import AP as mkAP

F32 = mybir.dt.float32
BF16 = mybir.dt.bfloat16
AF = mybir.ActivationFunctionType
ALU = mybir.AluOpType
AXX = mybir.AxisListType.X

NCORES = 8
B, S, D = 16, 2048, 1024
BPC = B // NCORES
NCH = D // 128
NG = S // 512
EPS = 1e-5
EXPSHIFT = 16.0

ENGINES = ("sync", "vector", "scalar", "tensor", "gpsimd")


class PB:
    def __init__(self, nc):
        self.nc = nc
        self.items = []
        self.counts = {}

    def now(self, sem):
        return self.counts.get(sem, 0)

    def do(self, eng, fn, wait=(), inc=()):
        wait = tuple((s, v) for s, v in wait if v > 0)
        inc = tuple(inc)
        self.items.append((eng, fn, wait, inc))
        out = {}
        for s, a in inc:
            self.counts[s] = self.counts.get(s, 0) + a
            out[s] = self.counts[s]
        return out

    def dma(self, queue, out, in_, sem, wait=(), **kw):
        def fn(eng):
            return eng.dma_start(out=out, in_=in_, **kw)
        m = self.do(queue, fn, wait=wait, inc=((sem, 16),))
        return m[sem]

    def barrier_wait(self, eng, sems):
        self.do(eng, lambda e: None, wait=sems)

    def realize(self, ctx):
        nc = self.nc
        handles = {}
        for eng, fn, wait, inc in self.items:
            for s, _ in list(wait) + list(inc):
                if s not in handles:
                    handles[s] = ctx.enter_context(nc.semaphore(s))
        block = ctx.enter_context(nc.Block())
        per_eng = {e: [] for e in ENGINES}
        for eng, fn, wait, inc in self.items:
            per_eng[eng].append((fn, wait, inc))

        def mk(items):
            def section(eng):
                for fn, wait, inc in items:
                    for s, v in wait:
                        eng.wait_ge(handles[s], v)
                    inst = fn(eng)
                    for s, a in inc:
                        assert inst is not None
                        inst.then_inc(handles[s], a)
            return section

        for e in ENGINES:
            if per_eng[e]:
                getattr(block, e)(mk(per_eng[e]))
        return handles


def alloc(ctx, nc, name, shape, dtype):
    return ctx.enter_context(nc.sbuf_tensor(name, shape, dtype))


def alloc_psum(ctx, nc, name, shape, dtype=F32):
    return ctx.enter_context(nc.psum_tensor(name, shape, dtype))


def part_swap_aps(tile, ncols):
    th = tile[:].tensor
    ev = mkAP(th, 0, [[2 * ncols, 64], [1, ncols]])
    od = mkAP(th, ncols, [[2 * ncols, 64], [1, ncols]])
    return ev, od


def emit_consts(pb, t):
    pb.do("gpsimd", lambda eng: eng.memset(t["ident"][:], 0.0),
          inc=(("cmz", 1),))
    pb.do("gpsimd", lambda eng: eng.memset(t["tri"][:], 1.0),
          inc=(("cmz", 1),))

    def f_id(eng):
        return eng.affine_select(
            out=t["ident"][:], in_=t["ident"][:], compare_op=ALU.not_equal,
            fill=1.0, base=0, pattern=[[-1, 128]], channel_multiplier=1)
    pb.do("gpsimd", f_id, wait=[("cmz", 2)], inc=(("const", 1),))

    def f_tri(eng):
        return eng.affine_select(
            out=t["tri"][:], in_=t["tri"][:], compare_op=ALU.is_ge,
            fill=0.0, base=0, pattern=[[1, 128]], channel_multiplier=-1)
    pb.do("gpsimd", f_tri, wait=[("cmz", 2)], inc=(("const", 1),))

    def f_ones(eng):
        eng.memset(t["ones_bf"][:], 1.0)
        return eng.memset(t["ones_b1"][:], 1.0)
    pb.do("vector", f_ones, inc=(("const", 1),))


def emit_ln(pb, t, xbf, xln, wait_x):
    ps_sum = t["psA"][0][0:1, :]
    ps_sq = t["psA"][1][0:1, :]
    for g in range(NG):
        mm0 = pb.now("lnmm")
        for ch in range(NCH):
            bi = ch % 2
            sl = slice(ch * S + g * 512, ch * S + (g + 1) * 512)

            def fsq(eng, sl=sl, bi=bi):
                return eng.activation(t["sq"][bi][:], xbf[:, sl], AF.Square)
            w = [("const", 3), wait_x]
            prev = pb.now("lnmm")
            if ch >= 2:
                w.append(("lnmm", prev - 1))
            sq_m = pb.do("scalar", fsq, wait=w, inc=(("lnsq", 1),))["lnsq"]

            def fmm(eng, sl=sl, bi=bi, ch=ch):
                eng.matmul(ps_sum, t["ones_bf"][:, 0:1], xbf[:, sl],
                           start=(ch == 0), stop=(ch == NCH - 1))
                return eng.matmul(ps_sq, t["ones_bf"][:, 0:1], t["sq"][bi][:],
                                  start=(ch == 0), stop=(ch == NCH - 1))
            w = [("lnsq", sq_m)]
            if ch == 0:
                w.append(("lnfv", pb.now("lnfv")))
                w.append(("lno", pb.now("lno")))
            pb.do("tensor", fmm, wait=w, inc=(("lnmm", 2),))

        gsl = slice(g * 512, (g + 1) * 512)

        vr = t["vr_view"]
        m2 = t["m2_view"]
        lr = t["nsc"][0:1, 0:512]

        def ffin_v1(eng, gsl=gsl, vr=vr):
            eng.tensor_scalar_mul(t["mrow"][0:1, gsl], ps_sum, 1.0 / D)
            return eng.tensor_scalar_mul(vr, ps_sq, 1.0 / D)
        c1 = pb.do("vector", ffin_v1,
                   wait=[("lnmm", pb.now("lnmm")), ("lnfin", pb.now("lnfin"))],
                   inc=(("lnc", 1),))["lnc"]

        def ffin_v2(eng, gsl=gsl, m2=m2):
            return eng.tensor_mul(m2, t["mrow"][0:1, gsl], t["mrow"][0:1, gsl])
        c2 = pb.do("vector", ffin_v2, wait=[("lnc", c1)], inc=(("lnc", 1),))["lnc"]

        def ffin_v3(eng, vr=vr, m2=m2):
            return eng.scalar_tensor_tensor(vr, vr, EPS, m2, ALU.add, ALU.subtract)
        vm = pb.do("vector", ffin_v3, wait=[("lnc", c2)], inc=(("lnfv", 1),))["lnfv"]

        def ffin_s1(eng, vr=vr, lr=lr):
            return eng.activation(lr, vr, AF.Ln)
        s1 = pb.do("scalar", ffin_s1, wait=[("lnfv", vm)], inc=(("lns1", 1),))["lns1"]

        def ffin_s2(eng, gsl=gsl, lr=lr):
            return eng.activation(t["rrow"][0:1, gsl], lr,
                                  AF.Exp, scale=np.float32(-0.5))
        pb.do("scalar", ffin_s2, wait=[("lns1", s1)], inc=(("lnfin", 1),))

    for g in range(NG):
        gsl = slice(g * 512, (g + 1) * 512)

        def fbc(eng, gsl=gsl):
            eng.matmul(t["psA"][0][:], t["ones_b1"][0:1, :], t["mrow"][0:1, gsl],
                       start=True, stop=True)
            return eng.matmul(t["psA"][1][:], t["ones_b1"][0:1, :], t["rrow"][0:1, gsl],
                              start=True, stop=True)
        w = [("lnfin", pb.now("lnfin"))]
        w.append(("lno", pb.now("lno")))
        bc = pb.do("tensor", fbc, wait=w, inc=(("lnbc", 1),))["lnbc"]

        for ch in range(NCH):
            sl = slice(ch * S + g * 512, ch * S + (g + 1) * 512)

            def fnorm1(eng, sl=sl):
                return eng.tensor_sub(t["nsc"][:], xbf[:, sl], t["psA"][0][:])
            n1 = pb.do("vector", fnorm1, wait=[("lnbc", bc), ("lno", pb.now("lno"))],
                       inc=(("lnoa", 1),))["lnoa"]

            def fnorm2(eng, sl=sl):
                return eng.tensor_mul(xln[:, sl], t["nsc"][:], t["psA"][1][:])
            pb.do("vector", fnorm2, wait=[("lnoa", n1)], inc=(("lno", 1),))
    return ("lno", pb.now("lno"))


def load_w_fm(pb, wt, w_dram, sem, wait=()):
    for kc in range(NCH):
        pb.dma("gpsimd", wt[:, kc * D:(kc + 1) * D],
               w_dram[kc * 128:(kc + 1) * 128, :], sem,
               wait=wait if kc == 0 else ())
    return pb.now(sem)


def build_attn_std(rope, causal, biases, inner_res, debug=None):
    nc = bass.Bass("TRN2", target_bir_lowering=False)
    x = nc.dram_tensor("x", [BPC, D, S], F32, kind="ExternalInput").ap()
    wq_d = nc.dram_tensor("wq", [D, D], F32, kind="ExternalInput").ap()
    wk_d = nc.dram_tensor("wk", [D, D], F32, kind="ExternalInput").ap()
    wv_d = nc.dram_tensor("wv", [D, D], F32, kind="ExternalInput").ap()
    wp_d = nc.dram_tensor("wp", [D, D], F32, kind="ExternalInput").ap()
    if biases:
        bq_d = nc.dram_tensor("bq", [128, NCH], F32, kind="ExternalInput").ap()
        bk_d = nc.dram_tensor("bk", [128, NCH], F32, kind="ExternalInput").ap()
        bv_d = nc.dram_tensor("bv", [1, D], F32, kind="ExternalInput").ap()
        bp_d = nc.dram_tensor("bp", [128, NCH], F32, kind="ExternalInput").ap()
    if rope:
        ta_d = nc.dram_tensor("ropeA", [128, S], F32, kind="ExternalInput").ap()
        tb_d = nc.dram_tensor("ropeB", [128, S], F32, kind="ExternalInput").ap()
        wqs_d = nc.dram_tensor("wqs", [D, D], F32, kind="ExternalInput").ap()
        wks_d = nc.dram_tensor("wks", [D, D], F32, kind="ExternalInput").ap()
        bqs_d = nc.dram_tensor("bqs", [128, NCH], F32, kind="ExternalInput").ap()
        bks_d = nc.dram_tensor("bks", [128, NCH], F32, kind="ExternalInput").ap()
    o = nc.dram_tensor("o", [BPC, D, S], F32, kind="ExternalOutput").ap()
    if debug in ("qkv", "attn"):
        dq = nc.dram_tensor("dq", [BPC, D, S], F32, kind="ExternalOutput").ap()
        dk = nc.dram_tensor("dk", [BPC, D, S], F32, kind="ExternalOutput").ap()
        dv = nc.dram_tensor("dv", [BPC, 128, 16 * 16 * 65], F32, kind="ExternalOutput").ap()

    pb = PB(nc)
    with ExitStack() as ctx:
        t = {}
        for nm, shp, dt in [
            ("ident", [128, 128], BF16), ("tri", [128, 128], BF16),
            ("ones_bf", [128, 1], BF16), ("ones_b1", [1, 128], BF16),
            ("mrow", [1, S], BF16), ("rrow", [1, S], BF16),
            ("nsc", [128, 512], F32),
        ]:
            t[nm] = alloc(ctx, nc, nm, shp, dt)
        t["sq"] = [alloc(ctx, nc, f"sq{i}", [128, 512], BF16) for i in range(2)]
        t["psA"] = [alloc_psum(ctx, nc, f"psA{i}", [128, 512]) for i in range(2)]
        ps_o = [alloc_psum(ctx, nc, f"psO{i}", [128, 65]) for i in range(4)]
        ps_tr = alloc_psum(ctx, nc, "psTr", [64, 128], BF16)

        xbf = alloc(ctx, nc, "xbf", [128, NCH * S], BF16)
        xln = alloc(ctx, nc, "xln", [128, NCH * S], BF16)
        qf = alloc(ctx, nc, "qf", [128, NCH * S], BF16)
        kf = alloc(ctx, nc, "kf", [128, NCH * S], BF16)
        vt = alloc(ctx, nc, "vt", [128, 16 * 16 * 65], BF16)
        yf = xbf
        wt = alloc(ctx, nc, "wt", [128, NCH * D], BF16)
        ubuf = [alloc(ctx, nc, f"ubuf{i}", [128, 512], BF16) for i in range(2)]
        obf = alloc(ctx, nc, "obf", [128, 64], BF16)
        rr = alloc(ctx, nc, "rr", [128, 1], F32)
        xres = alloc(ctx, nc, "xres", [128, 512], F32)
        ot = alloc(ctx, nc, "ot", [128, 512], F32)
        t["vr_view"] = xres[0:1, :]
        t["m2_view"] = ot[0:1, :]
        if biases:
            bq_t = alloc(ctx, nc, "bq_t", [128, NCH], F32)
            bk_t = alloc(ctx, nc, "bk_t", [128, NCH], F32)
            bv_t = alloc(ctx, nc, "bv_t", [1, D], BF16)
            bp_t = alloc(ctx, nc, "bp_t", [128, NCH], F32)
        if rope:
            ta = alloc(ctx, nc, "ta", [128, S], BF16)
            tb = alloc(ctx, nc, "tb", [128, S], BF16)
            rsw = alloc(ctx, nc, "rsw", [128, 512], BF16)
            ru1 = alloc(ctx, nc, "ru1", [128, 512], BF16)
            bqs_t = alloc(ctx, nc, "bqs_t", [128, NCH], F32)
            bks_t = alloc(ctx, nc, "bks_t", [128, NCH], F32)

        emit_consts(pb, t)
        if rope:
            pb.dma("gpsimd", ta[:], ta_d[:], "cstg")
            pb.dma("gpsimd", tb[:], tb_d[:], "cstg")
            pb.dma("sync", bqs_t[:], bqs_d[:], "cst")
            pb.dma("sync", bks_t[:], bks_d[:], "cst")
        if biases:
            pb.dma("sync", bq_t[:], bq_d[:], "cst")
            pb.dma("sync", bk_t[:], bk_d[:], "cst")
            pb.dma("sync", bp_t[:], bp_d[:], "cst")
            pb.dma("gpsimd", bv_t[:], bv_d[:], "cstg")
        cst_w = []
        if pb.now("cst"):
            cst_w.append(("cst", pb.now("cst")))
        if pb.now("cstg"):
            cst_w.append(("cstg", pb.now("cstg")))
        cst = cst_w[0] if cst_w else ("cst", 0)
        cstg = ("cstg", pb.now("cstg")) if pb.now("cstg") else ("cst", 0)

        HS = 64
        scale = 1.0 / 8.0

        for b in range(BPC):
            xw = [("ost", pb.now("ost"))] if b > 0 else []
            for ch in range(NCH):
                pb.dma("gpsimd", xbf[:, ch * S:(ch + 1) * S],
                       x[b, ch * 128:(ch + 1) * 128, :], "xld",
                       wait=xw if ch == 0 else ())
            lnw = ("xld", pb.now("xld"))
            ln_sem, ln_cnt = emit_ln(pb, t, xbf[:], xln[:], lnw)

            wld = load_w_fm(pb, wt, wq_d, "wld",
                            wait=[("pmm", pb.now("pmm"))] if b > 0 else ())

            def proj(dst, bias_t):
                for co in range(NCH):
                    for g in range(NG):
                        pi = (co * NG + g) % 2

                        def fmm(eng, co=co, g=g, pi=pi):
                            last = None
                            for kc in range(NCH):
                                last = eng.matmul(
                                    t["psA"][pi][:],
                                    wt[:, kc * D + co * 128: kc * D + co * 128 + 128],
                                    xln[:, kc * S + g * 512: kc * S + (g + 1) * 512],
                                    start=(kc == 0), stop=(kc == NCH - 1))
                            return last
                        w = [("wld", pb.now("wld")), (ln_sem, ln_cnt)]
                        w.append(("pev", max(0, pb.now("pmm") - 1)))
                        mm = pb.do("tensor", fmm, wait=w, inc=(("pmm", 1),))["pmm"]

                        dsl = slice(co * S + g * 512, co * S + (g + 1) * 512)

                        def fev(eng, dsl=dsl, pi=pi, co=co):
                            if bias_t is not None:
                                return eng.tensor_scalar_add(
                                    dst[:, dsl], t["psA"][pi][:], bias_t[:, co:co + 1])
                            return eng.tensor_copy(dst[:, dsl], t["psA"][pi][:])
                        pb.do("vector", fev, wait=[("pmm", mm), cst],
                              inc=(("pev", 1),))

            def proj_rope_swap(dst, bias_s):
                for co in range(NCH):
                    for g in range(NG):
                        pi = (co * NG + g) % 2

                        def fmm(eng, co=co, g=g, pi=pi):
                            last = None
                            for kc in range(NCH):
                                last = eng.matmul(
                                    t["psA"][pi][:],
                                    wt[:, kc * D + co * 128: kc * D + co * 128 + 128],
                                    xln[:, kc * S + g * 512: kc * S + (g + 1) * 512],
                                    start=(kc == 0), stop=(kc == NCH - 1))
                            return last
                        w = [("wld", pb.now("wld")), (ln_sem, ln_cnt)]
                        w.append(("pev", max(0, pb.now("pmm") - 1)))
                        mm = pb.do("tensor", fmm, wait=w, inc=(("pmm", 1),))["pmm"]

                        dsl = slice(co * S + g * 512, co * S + (g + 1) * 512)

                        def fev(eng, pi=pi, co=co):
                            if bias_s is not None:
                                return eng.tensor_scalar_add(
                                    rsw[:], t["psA"][pi][:], bias_s[:, co:co + 1])
                            return eng.tensor_copy(rsw[:], t["psA"][pi][:])
                        w2 = [("pmm", mm), cst, cstg, ("pev", pb.now("pev"))]
                        ev = pb.do("vector", fev, wait=w2, inc=(("pevs", 1),))["pevs"]

                        def frt1(eng, g=g, dsl=dsl):
                            gs = slice(g * 512, (g + 1) * 512)
                            eng.tensor_mul(rsw[:], rsw[:], tb[:, gs])
                            return eng.tensor_mul(ru1[:], dst[:, dsl], ta[:, gs])
                        r1 = pb.do("vector", frt1, wait=[("pevs", ev)],
                                   inc=(("prta", 1),))["prta"]

                        def frt2(eng, dsl=dsl):
                            return eng.tensor_add(dst[:, dsl], ru1[:], rsw[:])
                        pb.do("vector", frt2, wait=[("prta", r1)],
                              inc=(("pev", 1),))

            proj(qf, bq_t if biases else None)
            if rope:
                load_w_fm(pb, wt, wqs_d, "wld", wait=[("pmm", pb.now("pmm"))])
                proj_rope_swap(qf, bqs_t)
            load_w_fm(pb, wt, wk_d, "wld", wait=[("pmm", pb.now("pmm"))])
            proj(kf, bk_t if biases else None)
            if rope:
                load_w_fm(pb, wt, wks_d, "wld", wait=[("pmm", pb.now("pmm"))])
                proj_rope_swap(kf, bks_t)

            load_w_fm(pb, wt, wv_d, "wld", wait=[("pmm", pb.now("pmm"))])
            vw = [("ost", pb.now("ost"))] if b > 0 else []
            pb.do("vector", lambda eng: eng.memset(vt[:], 1.0),
                  wait=vw + [("const", 3)], inc=(("vms", 1),))
            for tt in range(16):
                for cg in range(2):
                    pi = (tt * 2 + cg) % 2

                    def fmm(eng, tt=tt, cg=cg, pi=pi):
                        for kc in range(NCH):
                            eng.matmul(
                                t["psA"][pi][:],
                                xln[:, kc * S + tt * 128: kc * S + (tt + 1) * 128],
                                wt[:, kc * D + cg * 512: kc * D + (cg + 1) * 512],
                                start=(kc == 0), stop=False)
                        if biases:
                            return eng.matmul(t["psA"][pi][:], t["ones_b1"][0:1, :],
                                              bv_t[0:1, cg * 512:(cg + 1) * 512],
                                              start=False, stop=True)
                        return eng.matmul(
                            t["psA"][pi][:],
                            xln[:, (NCH - 1) * S + tt * 128: (NCH - 1) * S + tt * 128 + 128],
                            wt[:, (NCH - 1) * D + cg * 512: NCH * D],
                            start=False, stop=True)
                    if not biases:
                        def fmm(eng, tt=tt, cg=cg, pi=pi):
                            last = None
                            for kc in range(NCH):
                                last = eng.matmul(
                                    t["psA"][pi][:],
                                    xln[:, kc * S + tt * 128: kc * S + (tt + 1) * 128],
                                    wt[:, kc * D + cg * 512: kc * D + (cg + 1) * 512],
                                    start=(kc == 0), stop=(kc == NCH - 1))
                            return last
                    w = [("wld", pb.now("wld")), (ln_sem, ln_cnt), ("vms", pb.now("vms")), cstg]
                    w.append(("pev", max(0, pb.now("pmm") - 1)))
                    mm = pb.do("tensor", fmm, wait=w, inc=(("pmm", 1),))["pmm"]

                    def fev(eng, tt=tt, cg=cg, pi=pi):
                        last = None
                        for hh in range(8):
                            h = cg * 8 + hh
                            last = eng.tensor_copy(
                                vt[:, (tt * 16 + h) * 65:(tt * 16 + h) * 65 + 64],
                                t["psA"][pi][:, hh * 64:(hh + 1) * 64])
                        return last
                    pb.do("vector", fev, wait=[("pmm", mm)],
                          inc=(("pev", 1),))

            qkv_done = ("pev", pb.now("pev"))

            if debug == "qkv":
                for ch in range(NCH):
                    for g in range(NG):
                        sl = slice(ch * S + g * 512, ch * S + (g + 1) * 512)

                        def fdq(eng, sl=sl):
                            return eng.tensor_copy(ot[:], qf[:, sl])
                        cp = pb.do("vector", fdq, wait=[qkv_done, ("ost", pb.now("ost"))],
                                   inc=(("ocp", 1),))["ocp"]
                        pb.dma("sync", dq[b, ch * 128:(ch + 1) * 128, g * 512:(g + 1) * 512],
                               ot[:], "ost", wait=[("ocp", cp)])

                        def fdk(eng, sl=sl):
                            return eng.tensor_copy(ot[:], kf[:, sl])
                        cp = pb.do("vector", fdk, wait=[("ost", pb.now("ost"))],
                                   inc=(("ocp", 1),))["ocp"]
                        pb.dma("sync", dk[b, ch * 128:(ch + 1) * 128, g * 512:(g + 1) * 512],
                               ot[:], "ost", wait=[("ocp", cp)])
                for blk in range(65):
                    bsl = slice(blk * 256, (blk + 1) * 256)

                    def fdv(eng, bsl=bsl):
                        return eng.tensor_copy(ot[:, 0:256], vt[:, bsl])
                    cp = pb.do("vector", fdv, wait=[("ost", pb.now("ost"))],
                               inc=(("ocp", 1),))["ocp"]
                    pb.dma("sync", dv[b, :, blk * 256:(blk + 1) * 256],
                           ot[:, 0:256], "ost", wait=[("ocp", cp)])
                continue

            for h in range(16):
                r0 = (h % 2) * 64
                cbase = (h // 2) * S
                for qt in range(NG):
                    kmax = 4 * qt + 4 if causal else 16
                    for kc in range(kmax):
                        spi = kc % 2

                        def fsm(eng, h=h, qt=qt, kc=kc, spi=spi, r0=r0, cbase=cbase):
                            return eng.matmul(
                                t["psA"][spi][:],
                                kf[r0:r0 + 64, cbase + kc * 128: cbase + (kc + 1) * 128],
                                qf[r0:r0 + 64, cbase + qt * 512: cbase + (qt + 1) * 512],
                                start=True, stop=True)
                        w = [qkv_done]
                        w.append(("aex", max(0, pb.now("asm") - 1)))
                        sm = pb.do("tensor", fsm, wait=w, inc=(("asm", 1),))["asm"]

                        def fex(eng, spi=spi, kc=kc):
                            return eng.activation(ubuf[kc % 2][:], t["psA"][spi][:],
                                                  AF.Exp, bias=np.float32(-EXPSHIFT), scale=np.float32(scale))
                        w = [("asm", sm), ("apv", max(0, pb.now("apv") - 1))]
                        ex = pb.do("scalar", fex, wait=w, inc=(("aex", 1),))["aex"]

                        need_mask = causal and kc >= 4 * qt
                        if need_mask:
                            j = kc - 4 * qt

                            def fmsk(eng, kc=kc, j=j):
                                return eng.tensor_mul(
                                    ubuf[kc % 2][:, j * 128:(j + 1) * 128],
                                    ubuf[kc % 2][:, j * 128:(j + 1) * 128],
                                    t["tri"][:])
                            msk = pb.do("vector", fmsk, wait=[("aex", ex)],
                                        inc=(("amsk", 1),))["amsk"]
                            uready = ("amsk", msk)
                        else:
                            uready = ("aex", ex)

                        def fpv(eng, h=h, qt=qt, kc=kc):
                            last = None
                            for qsub in range(4):
                                if causal and kc > 4 * qt + qsub:
                                    continue
                                stop_k = 4 * qt + qsub if causal else kmax - 1
                                last = eng.matmul(
                                    ps_o[qsub][:],
                                    ubuf[kc % 2][:, qsub * 128:(qsub + 1) * 128],
                                    vt[:, (kc * 16 + h) * 65:(kc * 16 + h) * 65 + 65],
                                    start=(kc == 0), stop=(kc == stop_k))
                            return last
                        w = [uready]
                        if kc == 0:
                            w.append(("aoc", pb.now("aoc")))
                        pb.do("tensor", fpv, wait=w, inc=(("apv", 1),))

                    for qsub in range(4):
                        def frc1(eng, qsub=qsub):
                            return eng.reciprocal(rr[:], ps_o[qsub][:, 64:65])
                        w = [("apv", pb.now("apv")), ("arc", pb.now("arc"))]
                        r1 = pb.do("vector", frc1, wait=w, inc=(("arca", 1),))["arca"]

                        def frc2(eng, qsub=qsub):
                            return eng.tensor_scalar_mul(obf[:], ps_o[qsub][:, 0:64],
                                                         rr[:, 0:1])
                        w = [("arca", r1), ("atr", pb.now("atr"))]
                        rc = pb.do("vector", frc2, wait=w, inc=(("arc", 1),))["arc"]

                        def ftr(eng):
                            return eng.transpose(ps_tr[:], obf[:], t["ident"][:])
                        w = [("arc", rc), ("const", 3), ("aoc", pb.now("aoc"))]
                        tr = pb.do("tensor", ftr, wait=w, inc=(("atr", 1),))["atr"]

                        def fyc(eng, h=h, qt=qt, qsub=qsub, r0=r0, cbase=cbase):
                            return eng.tensor_copy(
                                yf[r0:r0 + 64,
                                   cbase + qt * 512 + qsub * 128: cbase + qt * 512 + (qsub + 1) * 128],
                                ps_tr[:])
                        pb.do("vector", fyc, wait=[("atr", tr)], inc=(("aoc", 1),))

            attn_done = ("aoc", pb.now("aoc"))

            if debug == "attn":
                for ch in range(NCH):
                    for g in range(NG):
                        sl = slice(ch * S + g * 512, ch * S + (g + 1) * 512)

                        def fdy(eng, sl=sl):
                            return eng.tensor_copy(ot[:], yf[:, sl])
                        cp = pb.do("vector", fdy, wait=[attn_done, ("ost", pb.now("ost"))],
                                   inc=(("ocp", 1),))["ocp"]
                        pb.dma("sync", dq[b, ch * 128:(ch + 1) * 128, g * 512:(g + 1) * 512],
                               ot[:], "ost", wait=[("ocp", cp)])
                continue

            load_w_fm(pb, wt, wp_d, "wld", wait=[("pmm", pb.now("pmm"))])
            for co in range(NCH):
                for g in range(NG):
                    pi = (co * NG + g) % 2

                    def fmm(eng, co=co, g=g, pi=pi):
                        last = None
                        for kc in range(NCH):
                            last = eng.matmul(
                                t["psA"][pi][:],
                                wt[:, kc * D + co * 128: kc * D + co * 128 + 128],
                                yf[:, kc * S + g * 512: kc * S + (g + 1) * 512],
                                start=(kc == 0), stop=(kc == NCH - 1))
                        return last
                    w = [("wld", pb.now("wld")), attn_done]
                    w.append(("oev", max(0, pb.now("omm") - 1)))
                    mm = pb.do("tensor", fmm, wait=w, inc=(("omm", 1),))["omm"]

                    xrw = [("oev", pb.now("oev"))]
                    if co == 0 and g == 0:
                        xrw.append(attn_done)
                    xr = pb.dma("sync", xres[:],
                                x[b, co * 128:(co + 1) * 128, g * 512:(g + 1) * 512],
                                "xrl", wait=xrw)

                    def fev_v(eng, co=co, g=g, pi=pi):
                        bias_arg = bp_t[:, co:co + 1] if biases else 0.0
                        return eng.scalar_tensor_tensor(
                            ot[:], t["psA"][pi][:], bias_arg, xres[:],
                            ALU.add, ALU.add)
                    w = [("omm", mm), ("xrl", xr), cst]
                    w.append(("ost", 16 * max(0, pb.now("omm") - 1)))
                    if inner_res:
                        e1 = pb.do("vector", fev_v, wait=w, inc=(("oeva", 1),))["oeva"]

                        def fev_v2(eng, co=co, g=g):
                            return eng.tensor_add(
                                ot[:], ot[:],
                                xln[:, co * S + g * 512: co * S + (g + 1) * 512])
                        ev = pb.do("vector", fev_v2, wait=[("oeva", e1)],
                                   inc=(("oev", 1),))["oev"]
                    else:
                        ev = pb.do("vector", fev_v, wait=w, inc=(("oev", 1),))["oev"]
                    pb.dma("sync", o[b, co * 128:(co + 1) * 128, g * 512:(g + 1) * 512],
                           ot[:], "ost", wait=[("oev", ev)])

        pb.barrier_wait("sync", [("ost", pb.now("ost"))])
        pb.realize(ctx)
    return nc


def build_zproj():
    nc = bass.Bass("TRN2", target_bir_lowering=False)
    x = nc.dram_tensor("x", [BPC, D, S], F32, kind="ExternalInput").ap()
    z = nc.dram_tensor("z", [BPC, D, S], F32, kind="ExternalInput").ap()
    w = nc.dram_tensor("w", [D, D], F32, kind="ExternalInput").ap()
    o = nc.dram_tensor("o", [BPC, D, S], F32, kind="ExternalOutput").ap()

    pb = PB(nc)
    with ExitStack() as ctx:
        wt = alloc(ctx, nc, "wt", [128, NCH * D], BF16)
        zt = alloc(ctx, nc, "zt", [128, NCH * S], BF16)
        xt = alloc(ctx, nc, "xt", [128, 512], F32)
        ot = alloc(ctx, nc, "ot", [128, 512], F32)
        ps = [alloc_psum(ctx, nc, f"ps{i}", [128, 512]) for i in range(2)]

        for kc in range(NCH):
            pb.dma("gpsimd", wt[:, kc * D:(kc + 1) * D], w[kc * 128:(kc + 1) * 128, :], "wld")
        wld_done = pb.now("wld")

        it = 0
        for b in range(BPC):
            zw = (("mm", pb.now("mm")),) if b > 0 else ()
            for kc in range(NCH):
                pb.dma("gpsimd", zt[:, kc * S:(kc + 1) * S], z[b, kc * 128:(kc + 1) * 128, :], "zld",
                       wait=zw if kc == 0 else ())
            zld_done = pb.now("zld")
            for co in range(NCH):
                for g in range(NG):
                    pi = it % 2
                    xw = (("ev", it),) if it >= 1 else ()
                    xld = pb.dma("sync", xt[:], x[b, co * 128:(co + 1) * 128, g * 512:(g + 1) * 512],
                                 "xld", wait=xw)

                    def mm(eng, co=co, g=g, pi=pi):
                        last = None
                        for kc in range(NCH):
                            last = eng.matmul(
                                ps[pi][:],
                                wt[:, kc * D + co * 128: kc * D + co * 128 + 128],
                                zt[:, kc * S + g * 512: kc * S + (g + 1) * 512],
                                start=(kc == 0), stop=(kc == NCH - 1))
                        return last
                    mmw = [("wld", wld_done), ("zld", zld_done)]
                    if it >= 2:
                        mmw.append(("ev", it - 1))
                    mmm = pb.do("tensor", mm, wait=mmw, inc=(("mm", 1),))

                    def ev(eng, pi=pi):
                        return eng.tensor_add(ot[:], ps[pi][:], xt[:])
                    evw = [("mm", mmm["mm"]), ("xld", xld)]
                    if it >= 1:
                        evw.append(("st", 16 * it))
                    evm = pb.do("vector", ev, wait=evw, inc=(("ev", 1),))
                    pb.dma("sync", o[b, co * 128:(co + 1) * 128, g * 512:(g + 1) * 512],
                           ot[:], "st", wait=(("ev", evm["ev"]),))
                    it += 1

        pb.barrier_wait("sync", [("st", pb.now("st"))])
        pb.realize(ctx)
    return nc




def build_attn_rs():
    nc = bass.Bass("TRN2", target_bir_lowering=False)
    x = nc.dram_tensor("x", [BPC, D, S], F32, kind="ExternalInput").ap()
    wqkv_d = nc.dram_tensor("wqkv", [D, 3 * D], F32, kind="ExternalInput").ap()
    bqkv_d = nc.dram_tensor("bqkv", [64, 48], F32, kind="ExternalInput").ap()
    wp_d = nc.dram_tensor("wp", [D, D], F32, kind="ExternalInput").ap()
    bp_d = nc.dram_tensor("bp", [128, NCH], F32, kind="ExternalInput").ap()
    o = nc.dram_tensor("o", [BPC, D, S], F32, kind="ExternalOutput").ap()

    pb = PB(nc)
    HT = S // 2
    with ExitStack() as ctx:
        t = {}
        for nm, shp, dt in [
            ("ident", [128, 128], BF16), ("tri", [128, 128], BF16),
            ("ones_bf", [128, 1], BF16), ("ones_b1", [1, 128], BF16),
            ("nsc", [128, 512], F32),
        ]:
            t[nm] = alloc(ctx, nc, nm, shp, dt)
        t["sq"] = [alloc(ctx, nc, f"sq{i}", [128, 512], BF16) for i in range(2)]
        t["psA"] = [alloc_psum(ctx, nc, f"psA{i}", [128, 512]) for i in range(2)]
        ps_o = [alloc_psum(ctx, nc, f"psO{i}", [128, 65]) for i in range(4)]
        ps_tr = alloc_psum(ctx, nc, "psTr", [64, 128], BF16)
        ps_vt = alloc_psum(ctx, nc, "psVt", [128, 64], BF16)

        xbf = alloc(ctx, nc, "xbf", [128, NCH * S], BF16)
        xln = alloc(ctx, nc, "xln", [128, NCH * S], BF16)
        yf = xbf
        wt = xln[:, 0:NCH * D]
        qsp = alloc(ctx, nc, "qsp", [64, 16 * HT], BF16)
        ksp = alloc(ctx, nc, "ksp", [64, 16 * HT], BF16)
        vsp = alloc(ctx, nc, "vsp", [64, 16 * HT], BF16)
        t["mrow"] = qsp[0:1, 0:S]
        t["rrow"] = qsp[0:1, S:2 * S]
        wts = alloc(ctx, nc, "wts", [128, NCH * 64], BF16)
        vti = alloc(ctx, nc, "vti", [128, 16 * 65], BF16)
        bqkv_t = alloc(ctx, nc, "bqkv_t", [64, 48], F32)
        bp_t = alloc(ctx, nc, "bp_t", [128, NCH], F32)
        ubuf = [alloc(ctx, nc, f"ubuf{i}", [128, 512], BF16) for i in range(2)]
        obf = alloc(ctx, nc, "obf", [128, 64], BF16)
        rr = alloc(ctx, nc, "rr", [128, 1], F32)
        xres = alloc(ctx, nc, "xres", [128, 512], F32)
        ot = alloc(ctx, nc, "ot", [128, 512], F32)
        t["vr_view"] = xres[0:1, :]
        t["m2_view"] = ot[0:1, :]

        emit_consts(pb, t)
        pb.dma("sync", bqkv_t[:], bqkv_d[:], "cst")
        pb.dma("sync", bp_t[:], bp_d[:], "cst")
        cst = ("cst", pb.now("cst"))
        scale = 1.0 / 8.0

        for b in range(BPC):
            xw = [("ost", pb.now("ost"))] if b > 0 else []
            for ch in range(NCH):
                pb.dma("gpsimd", xbf[:, ch * S:(ch + 1) * S],
                       x[b, ch * 128:(ch + 1) * 128, :], "xld",
                       wait=xw if ch == 0 else ())
            lnw = ("xld", pb.now("xld"))
            ln_sem, ln_cnt = emit_ln(pb, t, xbf[:], xln[:], lnw)

            for half in range(2):
                t0 = half * HT
                for j in range(48):
                    g = j % 16
                    dst_t = (qsp, ksp, vsp)[j // 16]
                    ww = [("pmm", pb.now("pmm"))] if (j > 0 or half > 0 or b > 0) else []
                    if j == 0 and (half > 0 or b > 0):
                        ww = ww + [("asm", pb.now("asm")), ("atv", pb.now("atv"))]
                    for kc in range(NCH):
                        pb.dma("gpsimd", wts[:, kc * 64:(kc + 1) * 64],
                               wqkv_d[kc * 128:(kc + 1) * 128, j * 64:(j + 1) * 64],
                               "wld", wait=ww if kc == 0 else ())
                    wld = pb.now("wld")
                    for tg in range(2):
                        pi = (j * 2 + tg) % 2

                        def fmm(eng, j=j, tg=tg, pi=pi, t0=t0):
                            last = None
                            for kc in range(NCH):
                                last = eng.matmul(
                                    t["psA"][pi][0:64, :],
                                    wts[:, kc * 64:(kc + 1) * 64],
                                    xln[:, kc * S + t0 + tg * 512:
                                        kc * S + t0 + (tg + 1) * 512],
                                    start=(kc == 0), stop=(kc == NCH - 1))
                            return last
                        w = [("wld", wld), (ln_sem, ln_cnt)]
                        w.append(("pev", max(0, pb.now("pmm") - 1)))
                        mm = pb.do("tensor", fmm, wait=w, inc=(("pmm", 1),))["pmm"]

                        dst = mkAP(dst_t[:].tensor, tg * 4 * 2048 + g,
                                   [[16 * HT, 64], [2048, 4], [16, 128]])

                        def fev(eng, dst=dst, pi=pi, j=j):
                            return eng.tensor_scalar_add(
                                dst, t["psA"][pi][0:64, :], bqkv_t[:, j:j + 1])
                        pb.do("vector", fev, wait=[("pmm", mm), cst], inc=(("pev", 1),))

                qkv_done = ("pev", pb.now("pev"))

                q_th = qsp[:].tensor
                k_th = ksp[:].tensor
                v_th = vsp[:].tensor
                for hh in range(8):
                    h = half * 8 + hh
                    r0 = (h % 2) * 64
                    cbase = (h // 2) * S
                    vmsw = [("apv", pb.now("apv"))] if (h > 0 or b > 0) else []
                    pb.do("vector", lambda eng: eng.memset(vti[:], 1.0),
                          wait=vmsw + [qkv_done], inc=(("vms", 1),))
                    vms = pb.now("vms")
                    for kt in range(16):
                        src = vsp[0:64, hh * 2048 + kt * 128: hh * 2048 + (kt + 1) * 128]

                        def ftv(eng, src=src):
                            return eng.transpose(ps_vt[:], src, t["ident"][0:64, 0:64])
                        w = [qkv_done, ("const", 3), ("avt", pb.now("avt"))]
                        tv = pb.do("tensor", ftv, wait=w, inc=(("atv", 1),))["atv"]

                        def fve(eng, kt=kt):
                            return eng.tensor_copy(vti[:, kt * 65:kt * 65 + 64], ps_vt[:])
                        pb.do("vector", fve, wait=[("atv", tv), ("vms", vms)],
                              inc=(("avt", 1),))
                    vti_done = ("avt", pb.now("avt"))

                    for qt in range(NG):
                        kmax = 4 * qt + 4
                        for kc in range(kmax):
                            spi = kc % 2
                            lhsT = ksp[0:64, hh * 2048 + kc * 128: hh * 2048 + (kc + 1) * 128]
                            rhs = qsp[0:64, hh * 2048 + qt * 512: hh * 2048 + (qt + 1) * 512]

                            def fsm(eng, lhsT=lhsT, rhs=rhs, spi=spi):
                                return eng.matmul(t["psA"][spi][:], lhsT, rhs,
                                                  start=True, stop=True)
                            w = [qkv_done]
                            w.append(("aex", max(0, pb.now("asm") - 1)))
                            sm = pb.do("tensor", fsm, wait=w, inc=(("asm", 1),))["asm"]

                            def fex(eng, spi=spi, kc=kc):
                                return eng.activation(
                                    ubuf[kc % 2][:], t["psA"][spi][:], AF.Exp,
                                    bias=np.float32(-EXPSHIFT), scale=np.float32(scale))
                            w = [("asm", sm), ("apv", max(0, pb.now("apv") - 1))]
                            ex = pb.do("scalar", fex, wait=w, inc=(("aex", 1),))["aex"]

                            if kc >= 4 * qt:
                                jj = kc - 4 * qt

                                def fmsk(eng, kc=kc, jj=jj):
                                    return eng.tensor_mul(
                                        ubuf[kc % 2][:, jj * 128:(jj + 1) * 128],
                                        ubuf[kc % 2][:, jj * 128:(jj + 1) * 128],
                                        t["tri"][:])
                                msk = pb.do("vector", fmsk, wait=[("aex", ex)],
                                            inc=(("amsk", 1),))["amsk"]
                                uready = ("amsk", msk)
                            else:
                                uready = ("aex", ex)

                            def fpv(eng, qt=qt, kc=kc):
                                last = None
                                for qsub in range(4):
                                    if kc > 4 * qt + qsub:
                                        continue
                                    last = eng.matmul(
                                        ps_o[qsub][:],
                                        ubuf[kc % 2][:, qsub * 128:(qsub + 1) * 128],
                                        vti[:, kc * 65:kc * 65 + 65],
                                        start=(kc == 0), stop=(kc == 4 * qt + qsub))
                                return last
                            w = [uready, vti_done]
                            if kc == 0:
                                w.append(("aoc", pb.now("aoc")))
                            pb.do("tensor", fpv, wait=w, inc=(("apv", 1),))

                        for qsub in range(4):
                            def frc1(eng, qsub=qsub):
                                return eng.reciprocal(rr[:], ps_o[qsub][:, 64:65])
                            w = [("apv", pb.now("apv")), ("arc", pb.now("arc"))]
                            r1 = pb.do("vector", frc1, wait=w, inc=(("arca", 1),))["arca"]

                            def frc2(eng, qsub=qsub):
                                return eng.tensor_scalar_mul(obf[:], ps_o[qsub][:, 0:64],
                                                             rr[:, 0:1])
                            w = [("arca", r1), ("atr", pb.now("atr"))]
                            rc = pb.do("vector", frc2, wait=w, inc=(("arc", 1),))["arc"]

                            def ftr(eng):
                                return eng.transpose(ps_tr[:], obf[:], t["ident"][:])
                            w = [("arc", rc), ("const", 3), ("aoc", pb.now("aoc"))]
                            tr = pb.do("tensor", ftr, wait=w, inc=(("atr", 1),))["atr"]

                            def fyc(eng, h=h, qt=qt, qsub=qsub, r0=r0, cbase=cbase):
                                return eng.tensor_copy(
                                    yf[r0:r0 + 64,
                                       cbase + qt * 512 + qsub * 128:
                                       cbase + qt * 512 + (qsub + 1) * 128],
                                    ps_tr[:])
                            pb.do("vector", fyc, wait=[("atr", tr)], inc=(("aoc", 1),))

            attn_done = ("aoc", pb.now("aoc"))

            for kc in range(NCH):
                pb.dma("gpsimd", wt[:, kc * D:(kc + 1) * D],
                       wp_d[kc * 128:(kc + 1) * 128, :], "wpl",
                       wait=[attn_done] if kc == 0 else ())
            wpl = pb.now("wpl")
            for co in range(NCH):
                for g in range(NG):
                    pi = (co * NG + g) % 2

                    def fmm(eng, co=co, g=g, pi=pi):
                        last = None
                        for kc in range(NCH):
                            last = eng.matmul(
                                t["psA"][pi][:],
                                wt[:, kc * D + co * 128: kc * D + co * 128 + 128],
                                yf[:, kc * S + g * 512: kc * S + (g + 1) * 512],
                                start=(kc == 0), stop=(kc == NCH - 1))
                        return last
                    w = [("wpl", wpl), attn_done]
                    w.append(("oev", max(0, pb.now("omm") - 1)))
                    mm = pb.do("tensor", fmm, wait=w, inc=(("omm", 1),))["omm"]

                    xrw = [("oev", pb.now("oev"))]
                    if co == 0 and g == 0:
                        xrw.append(attn_done)
                    xr = pb.dma("sync", xres[:],
                                x[b, co * 128:(co + 1) * 128, g * 512:(g + 1) * 512],
                                "xrl", wait=xrw)

                    def fev_v(eng, co=co, pi=pi):
                        return eng.scalar_tensor_tensor(
                            ot[:], t["psA"][pi][:], bp_t[:, co:co + 1], xres[:],
                            ALU.add, ALU.add)
                    w = [("omm", mm), ("xrl", xr), cst]
                    w.append(("ost", 16 * max(0, pb.now("omm") - 1)))
                    ev = pb.do("vector", fev_v, wait=w, inc=(("oev", 1),))["oev"]
                    pb.dma("sync", o[b, co * 128:(co + 1) * 128, g * 512:(g + 1) * 512],
                           ot[:], "ost", wait=[("oev", ev)])

        pb.barrier_wait("sync", [("ost", pb.now("ost"))])
        pb.realize(ctx)
    return nc




def build_moe():
    nc = bass.Bass("TRN2", target_bir_lowering=False)
    x = nc.dram_tensor("x", [BPC, D, S], F32, kind="ExternalInput").ap()
    wg_d = nc.dram_tensor("wg", [D, 8], F32, kind="ExternalInput").ap()
    bg_d = nc.dram_tensor("bg", [1, 8], F32, kind="ExternalInput").ap()
    w1_d = nc.dram_tensor("w1", [8, D, 4096], F32, kind="ExternalInput").ap()
    b1_d = nc.dram_tensor("b1", [128, 8 * 32], F32, kind="ExternalInput").ap()
    w2_d = nc.dram_tensor("w2", [8, 4096, D], F32, kind="ExternalInput").ap()
    b2_d = nc.dram_tensor("b2", [8, 1, D], F32, kind="ExternalInput").ap()
    o = nc.dram_tensor("o", [BPC, D, S], F32, kind="ExternalOutput").ap()

    pb = PB(nc)
    NMC = 16
    with ExitStack() as ctx:
        t = {}
        xbf = alloc(ctx, nc, "xbf", [128, NCH * S], BF16)
        xln = alloc(ctx, nc, "xln", [128, NCH * S], BF16)
        yacc = xbf
        w1t = alloc(ctx, nc, "w1t", [128, NCH * 2048], BF16)
        w2t = alloc(ctx, nc, "w2t", [128, NMC * D], BF16)
        hstore = alloc(ctx, nc, "hst", [128, NMC * 512], BF16)
        for nm, shp, dt in [("ones_bf", [128, 1], BF16), ("ones_b1", [1, 128], BF16),
                            ("nsc", [128, 512], F32),
                            ("mrow", [1, S], BF16), ("rrow", [1, S], BF16),
                            ("ident", [128, 128], BF16)]:
            t[nm] = alloc(ctx, nc, nm, shp, dt)
        t["sq"] = [alloc(ctx, nc, f"sq{i}", [128, 512], BF16) for i in range(2)]
        t["psA"] = [alloc_psum(ctx, nc, f"psA{i}", [128, 512]) for i in range(2)]
        ps_t2 = alloc_psum(ctx, nc, "psT2", [128, 128], BF16)
        wgt = alloc(ctx, nc, "wgt", [128, NCH * 8], BF16)
        bgt = alloc(ctx, nc, "bgt", [1, 8], BF16)
        b1t = alloc(ctx, nc, "b1t", [128, 8 * 32], F32)
        b2r = alloc(ctx, nc, "b2r", [1, D], BF16)
        gsel = alloc(ctx, nc, "gsel", [128, 16 * 8], F32)
        nmx = alloc(ctx, nc, "nmx", [128, 1], F32)
        ssum = alloc(ctx, nc, "ssum", [128, 1], F32)
        rp = alloc(ctx, nc, "rp", [128, 1], F32)
        esc = alloc(ctx, nc, "esc", [128, 8], F32)
        msk = alloc(ctx, nc, "msk", [128, 8], F32)
        xres = alloc(ctx, nc, "xres", [128, 512], F32)
        ot = alloc(ctx, nc, "ot", [128, 512], F32)
        t["vr_view"] = xres[0:1, :]
        t["m2_view"] = ot[0:1, :]

        pb.do("gpsimd", lambda eng: eng.memset(t["ident"][:], 0.0), inc=(("cmz", 1),))

        def f_id(eng):
            return eng.affine_select(
                out=t["ident"][:], in_=t["ident"][:], compare_op=ALU.not_equal,
                fill=1.0, base=0, pattern=[[-1, 128]], channel_multiplier=1)
        pb.do("gpsimd", f_id, wait=[("cmz", 1)], inc=(("const", 1),))

        def f_ones(eng):
            eng.memset(t["ones_bf"][:], 1.0)
            return eng.memset(t["ones_b1"][:], 1.0)
        pb.do("vector", f_ones, inc=(("const", 1),))
        pb.do("gpsimd", lambda eng: eng.nop(), wait=[("const", 2)], inc=(("const", 1),))

        for kc in range(NCH):
            pb.dma("gpsimd", wgt[:, kc * 8:(kc + 1) * 8],
                   wg_d[kc * 128:(kc + 1) * 128, :], "cstg")
        pb.dma("gpsimd", bgt[:], bg_d[:], "cstg")
        pb.dma("sync", b1t[:], b1_d[:], "cst")
        cstg = ("cstg", pb.now("cstg"))
        cst = ("cst", pb.now("cst"))

        for b in range(BPC):
            xw = [("ost", pb.now("ost"))] if b > 0 else []
            for ch in range(NCH):
                pb.dma("gpsimd", xbf[:, ch * S:(ch + 1) * S],
                       x[b, ch * 128:(ch + 1) * 128, :], "xld",
                       wait=xw if ch == 0 else ())
            lnw = ("xld", pb.now("xld"))
            ln_sem, ln_cnt = emit_ln(pb, t, xbf[:], xln[:], lnw)
            banks = [( ln_sem, ln_cnt), (ln_sem, ln_cnt)]

            pb.do("vector", lambda eng: eng.memset(yacc[:], 0.0),
                  wait=[(ln_sem, ln_cnt)], inc=(("yz", 1),))
            yz = ("yz", pb.now("yz"))

            for tt in range(16):
                def fgm(eng, tt=tt):
                    for kc in range(NCH):
                        eng.matmul(t["psA"][0][:, 0:8],
                                   xln[:, kc * S + tt * 128: kc * S + (tt + 1) * 128],
                                   wgt[:, kc * 8:(kc + 1) * 8],
                                   start=(kc == 0), stop=False)
                    return eng.matmul(t["psA"][0][:, 0:8], t["ones_b1"][0:1, :],
                                      bgt[0:1, :], start=False, stop=True)
                w = [(ln_sem, ln_cnt), cstg, ("const", 3), banks[0]]
                gm = pb.do("tensor", fgm, wait=w, inc=(("gmm", 1),))["gmm"]

                def fr1(eng):
                    return eng.reduce_max(nmx[:], t["psA"][0][:, 0:8], axis=AXX,
                                          negate=True)
                r1 = pb.do("vector", fr1, wait=[("gmm", gm)], inc=(("gr1", 1),))["gr1"]

                def fr2(eng):
                    return eng.activation(esc[:], t["psA"][0][:, 0:8], AF.Exp,
                                          bias=nmx[:, 0:1], accum_out=ssum[:, 0:1])
                r2 = pb.do("scalar", fr2, wait=[("gr1", r1)], inc=(("gr2", 1),))["gr2"]

                def fr3(eng):
                    return eng.reciprocal(rp[:], ssum[:, 0:1])
                r3 = pb.do("vector", fr3, wait=[("gr2", r2)], inc=(("gr3", 1),))["gr3"]

                def fr4(eng):
                    return eng.tensor_scalar(msk[:], t["psA"][0][:, 0:8],
                                             nmx[:, 0:1], 0.0, ALU.add, ALU.is_ge)
                r4 = pb.do("vector", fr4, wait=[("gr3", r3)], inc=(("gr4", 1),))["gr4"]

                def fr5(eng, tt=tt):
                    return eng.tensor_scalar_mul(gsel[:, tt * 8:(tt + 1) * 8],
                                                 msk[:], rp[:, 0:1])
                pb.do("vector", fr5, wait=[("gr4", r4)], inc=(("grt", 1),))
                banks[0] = ("gr4", pb.now("gr4"))
            routing_done = ("grt", pb.now("grt"))

            for e in range(8):
                pb.dma("gpsimd", b2r[:], b2_d[e], "b2l",
                       wait=[("yev", pb.now("yev"))])
                for half in range(2):
                    m0 = half * NMC
                    ww = [("hmm", pb.now("hmm"))]
                    for kc in range(NCH):
                        pb.dma("gpsimd", w1t[:, kc * 2048:(kc + 1) * 2048],
                               w1_d[e, kc * 128:(kc + 1) * 128,
                                    half * 2048:(half + 1) * 2048],
                               "w1l", wait=ww if kc == 0 else ())
                    w1l = pb.now("w1l")
                    w2w = [("ymm", pb.now("ymm"))]
                    for mc in range(NMC):
                        pb.dma("gpsimd", w2t[:, mc * D:(mc + 1) * D],
                               w2_d[e, half * 2048 + mc * 128:
                                    half * 2048 + (mc + 1) * 128, :],
                               "w2l", wait=w2w if mc == 0 else ())
                    w2l = pb.now("w2l")

                    for tg in range(NG):
                        for mc in range(NMC):
                            pi = mc % 2

                            def fhm(eng, mc=mc, tg=tg, pi=pi):
                                last = None
                                for kc in range(NCH):
                                    last = eng.matmul(
                                        t["psA"][pi][:],
                                        w1t[:, kc * 2048 + mc * 128: kc * 2048 + (mc + 1) * 128],
                                        xln[:, kc * S + tg * 512: kc * S + (tg + 1) * 512],
                                        start=(kc == 0), stop=(kc == NCH - 1))
                                return last
                            w = [("w1l", w1l), (ln_sem, ln_cnt), banks[pi]]
                            hm = pb.do("tensor", fhm, wait=w, inc=(("hmm", 1),))["hmm"]

                            def fhe(eng, mc=mc, pi=pi, e=e, m0=m0):
                                return eng.activation(
                                    hstore[:, mc * 512:(mc + 1) * 512],
                                    t["psA"][pi][:], AF.Relu,
                                    bias=b1t[:, e * 32 + m0 + mc: e * 32 + m0 + mc + 1])
                            w = [("hmm", hm), cst, ("ymm", pb.now("ymm"))]
                            pb.do("scalar", fhe, wait=w, inc=(("hev", 1),))
                            banks[pi] = ("hev", pb.now("hev"))

                        hdone = ("hev", pb.now("hev"))
                        for ts_ in range(4):
                            for cg in range(2):
                                pi = (ts_ * 2 + cg) % 2

                                def fym(eng, ts_=ts_, cg=cg, pi=pi):
                                    last = None
                                    for mc in range(NMC):
                                        last = eng.matmul(
                                            t["psA"][pi][:],
                                            hstore[:, mc * 512 + ts_ * 128: mc * 512 + (ts_ + 1) * 128],
                                            w2t[:, mc * D + cg * 512: mc * D + (cg + 1) * 512],
                                            start=(mc == 0), stop=(mc == NMC - 1))
                                    return last
                                w = [("w2l", w2l), hdone, banks[pi]]
                                ym = pb.do("tensor", fym, wait=w, inc=(("ymm", 1),))["ymm"]

                                tt = tg * 4 + ts_

                                def fye(eng, tt=tt, cg=cg, pi=pi, e=e):
                                    ysl = yacc[:, tt * D + cg * 512: tt * D + (cg + 1) * 512]
                                    return eng.scalar_tensor_tensor(
                                        ysl, t["psA"][pi][:],
                                        gsel[:, tt * 8 + e: tt * 8 + e + 1],
                                        ysl, ALU.mult, ALU.add)
                                w = [("ymm", ym), ("b2l", pb.now("b2l")), yz,
                                     routing_done]
                                pb.do("vector", fye, wait=w, inc=(("yev", 1),))
                                banks[pi] = ("yev", pb.now("yev"))
                for tg in range(NG):
                    for ts_ in range(4):
                        for cg in range(2):
                            pi = (ts_ * 2 + cg) % 2
                            tt = tg * 4 + ts_

                            def fbm(eng, cg=cg, pi=pi):
                                return eng.matmul(
                                    t["psA"][pi][:], t["ones_b1"][0:1, :],
                                    b2r[0:1, cg * 512:(cg + 1) * 512],
                                    start=True, stop=True)
                            w = [("b2l", pb.now("b2l")), banks[pi]]
                            bm = pb.do("tensor", fbm, wait=w, inc=(("ymm", 1),))["ymm"]

                            def fbe(eng, tt=tt, cg=cg, pi=pi, e=e):
                                ysl = yacc[:, tt * D + cg * 512: tt * D + (cg + 1) * 512]
                                return eng.scalar_tensor_tensor(
                                    ysl, t["psA"][pi][:],
                                    gsel[:, tt * 8 + e: tt * 8 + e + 1],
                                    ysl, ALU.mult, ALU.add)
                            w = [("ymm", bm)]
                            pb.do("vector", fbe, wait=w, inc=(("yev", 1),))
                            banks[pi] = ("yev", pb.now("yev"))

            experts_done = ("yev", pb.now("yev"))

            for co in range(NCH):
                for g in range(NG):
                    xrw = [("oev", pb.now("oev"))]
                    if co == 0 and g == 0:
                        xrw.append(experts_done)
                    xr = pb.dma("sync", xres[:],
                                x[b, co * 128:(co + 1) * 128, g * 512:(g + 1) * 512],
                                "xrl", wait=xrw)
                    for ts_ in range(4):
                        tt = g * 4 + ts_

                        def ftr(eng, tt=tt, co=co):
                            return eng.transpose(
                                ps_t2[:], yacc[:, tt * D + co * 128: tt * D + (co + 1) * 128],
                                t["ident"][:])
                        w = [experts_done, ("const", 3), ("oevp", pb.now("oevp"))]
                        tr = pb.do("tensor", ftr, wait=w, inc=(("otr", 1),))["otr"]

                        def foe(eng, ts_=ts_):
                            return eng.tensor_copy(ot[:, ts_ * 128:(ts_ + 1) * 128],
                                                   ps_t2[:])
                        pb.do("vector", foe, wait=[("otr", tr),
                                                   ("ost", 16 * max(0, pb.now("oev")))],
                              inc=(("oevp", 1),))

                    def fadd(eng):
                        return eng.tensor_add(ot[:], ot[:], xres[:])
                    w = [("oevp", pb.now("oevp")), ("xrl", xr)]
                    ev = pb.do("vector", fadd, wait=w, inc=(("oev", 1),))["oev"]
                    pb.dma("sync", o[b, co * 128:(co + 1) * 128, g * 512:(g + 1) * 512],
                           ot[:], "ost", wait=[("oev", ev)])

        pb.barrier_wait("sync", [("ost", pb.now("ost"))])
        pb.realize(ctx)
    return nc






I4 = 2816
NB4 = I4 // 16


def emit_fp4q(pb, t, src, dst, N, wait=()):
    nb = N // 16
    A = t["qA"][:, 0:N]
    Bv = t["qB"][:, 0:N]
    C = t["qC"][:, 0:N]
    Dv = t["qD"][:, 0:N]
    am = t["qam"][:, 0:nb]
    scl = t["qscl"][:, 0:nb]
    rsc = t["qrsc"][:, 0:nb]
    sem = "fq"
    I32 = mybir.dt.int32

    def st(fn, w=()):
        return pb.do("vector", fn, wait=list(w) + [(sem, pb.now(sem))],
                     inc=((sem, 1),))[sem]

    def bcast16(v):
        return mkAP(v.tensor, v.offset, [list(v.ap[0]), [1, nb], [0, 16]])

    st(lambda eng: eng.reduce_max(
        am, src.rearrange("p (a b) -> p a b", b=16), axis=AXX,
        apply_absolute_value=True), w=list(wait))
    st(lambda eng: eng.tensor_scalar(scl, am, 1.0 / 6.0, 1e-12, ALU.mult, ALU.max))
    st(lambda eng: eng.reciprocal(rsc, scl))
    st(lambda eng: eng.tensor_tensor(
        A.rearrange("p (a b) -> p a b", b=16),
        src.rearrange("p (a b) -> p a b", b=16), bcast16(rsc), ALU.mult))
    st(lambda eng: eng.tensor_single_scalar(
        Bv.bitcast(I32), A.bitcast(I32), 0x7FFFFFFF, ALU.bitwise_and))
    st(lambda eng: eng.tensor_single_scalar(C, Bv, 0.25, ALU.is_gt))
    st(lambda eng: eng.tensor_single_scalar(Dv, Bv, 0.75, ALU.is_gt))
    st(lambda eng: eng.tensor_tensor(C, C, Dv, ALU.add))
    st(lambda eng: eng.tensor_single_scalar(C, C, 0.5, ALU.mult))
    st(lambda eng: eng.tensor_single_scalar(Bv, Bv, 1.0, ALU.is_lt))
    st(lambda eng: eng.tensor_single_scalar(
        Dv.bitcast(I32), A.bitcast(I32), 0x1FFFFF, ALU.add))
    st(lambda eng: eng.tensor_single_scalar(
        Dv.bitcast(I32), Dv.bitcast(I32), -4194304, ALU.bitwise_and))
    st(lambda eng: eng.tensor_single_scalar(
        A.bitcast(I32), A.bitcast(I32), -2147483648, ALU.bitwise_and))
    st(lambda eng: eng.tensor_tensor(
        C.bitcast(I32), C.bitcast(I32), A.bitcast(I32), ALU.bitwise_or))
    st(lambda eng: eng.copy_predicated(Dv, Bv.bitcast(I32), C))
    st(lambda eng: eng.tensor_tensor(
        dst.rearrange("p (a b) -> p a b", b=16),
        Dv.rearrange("p (a b) -> p a b", b=16), bcast16(scl), ALU.mult))
    return (sem, pb.now(sem))


def build_ffn4():
    nc = bass.Bass("TRN2", target_bir_lowering=False)
    x = nc.dram_tensor("x", [BPC, S, D], F32, kind="ExternalInput").ap()
    wg_d = nc.dram_tensor("wg", [I4, D], F32, kind="ExternalInput").ap()
    wu_d = nc.dram_tensor("wu", [I4, D], F32, kind="ExternalInput").ap()
    wd_d = nc.dram_tensor("wd", [D, I4], F32, kind="ExternalInput").ap()
    o = nc.dram_tensor("o", [BPC, S, D], F32, kind="ExternalOutput").ap()
    wgT = nc.dram_tensor("wgT", [D, I4], BF16, kind="Internal").ap()
    wuT = nc.dram_tensor("wuT", [D, I4], BF16, kind="Internal").ap()
    wdT = nc.dram_tensor("wdT", [I4, D], BF16, kind="Internal").ap()

    pb = PB(nc)
    with ExitStack() as ctx:
        t = {}
        for nm, shp, dt in [
            ("ident", [128, 128], BF16),
            ("qA", [128, I4], F32), ("qB", [128, I4], F32),
            ("qC", [128, I4], F32), ("qD", [128, I4], F32),
            ("qam", [128, NB4], F32), ("qscl", [128, NB4], F32),
            ("qrsc", [128, NB4], F32),
        ]:
            t[nm] = alloc(ctx, nc, nm, shp, dt)
        psq = [alloc_psum(ctx, nc, f"psq{i}", [128, 512]) for i in range(2)]
        ps_t2 = alloc_psum(ctx, nc, "psT2", [128, 128], BF16)
        xt = alloc(ctx, nc, "xt", [128, D], F32)
        xlt = alloc(ctx, nc, "xlt", [128, D], F32)
        xq = alloc(ctx, nc, "xq", [128, D], BF16)
        xqf = alloc(ctx, nc, "xqf", [128, NCH * S], BF16)
        wslf = alloc(ctx, nc, "wslf", [128, I4], F32)
        wqs = alloc(ctx, nc, "wqs", [128, I4], BF16)
        tscr = alloc(ctx, nc, "tscr", [128, 128], BF16)
        wgs = alloc(ctx, nc, "wgs", [128, NCH * 512], BF16)
        wus = alloc(ctx, nc, "wus", [128, NCH * 512], BF16)
        wds = [alloc(ctx, nc, f"wds{i}", [128, 22 * 512], BF16) for i in range(2)]
        ht = alloc(ctx, nc, "ht", [128, I4], F32)
        hq = alloc(ctx, nc, "hq", [128, I4], BF16)
        hqf = alloc(ctx, nc, "hqf", [128, 22 * 128], BF16)
        ott = alloc(ctx, nc, "ott", [128, D], F32)
        lnm = alloc(ctx, nc, "lnm", [128, 1], F32)
        lnv = alloc(ctx, nc, "lnv", [128, 1], F32)
        lns = alloc(ctx, nc, "lns", [128, 1], F32)

        pb.do("gpsimd", lambda eng: eng.memset(t["ident"][:], 0.0), inc=(("cmz", 1),))

        def f_id(eng):
            return eng.affine_select(
                out=t["ident"][:], in_=t["ident"][:], compare_op=ALU.not_equal,
                fill=1.0, base=0, pattern=[[-1, 128]], channel_multiplier=1)
        pb.do("gpsimd", f_id, wait=[("cmz", 1)], inc=(("const", 1),))
        cst = ("const", 1)

        def quant_transpose(w_src, n_rows, k_cols, dstT, tag):
            for ns in range(n_rows // 128):
                lw = pb.dma("sync", wslf[:, 0:k_cols],
                            w_src[ns * 128:(ns + 1) * 128, :], "qwl",
                            wait=[("fq", pb.now("fq"))])
                qm = emit_fp4q(pb, t, wslf[:, 0:k_cols], wqs[:, 0:k_cols], k_cols,
                               wait=[("qwl", lw)])
                for kc in range(k_cols // 128):
                    def ftr(eng, kc=kc):
                        return eng.transpose(ps_t2[:], wqs[:, kc * 128:(kc + 1) * 128],
                                             t["ident"][:])
                    w = [qm, cst, ("qwe", pb.now("qwe"))]
                    tr = pb.do("tensor", ftr, wait=w, inc=(("qwt", 1),))["qwt"]

                    def fev(eng):
                        return eng.tensor_copy(tscr[:], ps_t2[:])
                    ev = pb.do("vector", fev,
                               wait=[("qwt", tr), ("qws", 16 * max(0, pb.now("qwe")))],
                               inc=(("qwe", 1),))["qwe"]
                    pb.dma("sync", dstT[kc * 128:(kc + 1) * 128, ns * 128:(ns + 1) * 128],
                           tscr[:], "qws", wait=[("qwe", ev)])
            return ("qws", pb.now("qws"))

        quant_transpose(wg_d, I4, D, wgT, "wg")
        quant_transpose(wu_d, I4, D, wuT, "wu")
        wq_done = quant_transpose(wd_d, D, I4, wdT, "wd")
        wq_all = ("qws", pb.now("qws"))

        NGI = [512, 512, 512, 512, 512, 256]

        for b in range(BPC):
            for tt in range(16):
                xw = [("l10", pb.now("l10"))]
                if b > 0 and tt == 0:
                    xw.append(("ost", pb.now("ost")))
                lx = pb.dma("sync", xt[:], x[b, tt * 128:(tt + 1) * 128, :], "xld",
                            wait=xw)

                def fl1(eng):
                    return eng.reduce_sum(lnm[:], xt[:], axis=AXX)
                l1 = pb.do("vector", fl1,
                           wait=[("xld", lx), ("fq", pb.now("fq")),
                                 ("fme", pb.now("fme"))],
                           inc=(("l1", 1),))["l1"]

                def fl2(eng):
                    return eng.tensor_mul(t["qA"][:, 0:D], xt[:], xt[:])
                l2 = pb.do("vector", fl2, wait=[("l1", l1)], inc=(("l2", 1),))["l2"]

                def fl3(eng):
                    return eng.reduce_sum(lnv[:], t["qA"][:, 0:D], axis=AXX)
                l3 = pb.do("vector", fl3, wait=[("l2", l2)], inc=(("l3", 1),))["l3"]

                def fl4(eng):
                    return eng.tensor_single_scalar(lnm[:], lnm[:], 1.0 / D, ALU.mult)
                l4 = pb.do("vector", fl4, wait=[("l3", l3)], inc=(("l4", 1),))["l4"]

                def fl5(eng):
                    return eng.tensor_mul(lns[:], lnm[:], lnm[:])
                l5 = pb.do("vector", fl5, wait=[("l4", l4)], inc=(("l5", 1),))["l5"]

                def fl6(eng):
                    return eng.scalar_tensor_tensor(lnv[:], lnv[:], 1.0 / D, lns[:],
                                                    ALU.mult, ALU.subtract)
                l6 = pb.do("vector", fl6, wait=[("l5", l5)], inc=(("l6", 1),))["l6"]

                def fl7(eng):
                    return eng.tensor_single_scalar(lnv[:], lnv[:], EPS, ALU.add)
                l7 = pb.do("vector", fl7, wait=[("l6", l6)], inc=(("l7", 1),))["l7"]

                def fl8(eng):
                    return eng.activation(lns[:], lnv[:], AF.Ln)
                l8 = pb.do("scalar", fl8, wait=[("l7", l7)], inc=(("l8", 1),))["l8"]

                def fl9(eng):
                    return eng.activation(lnv[:], lns[:], AF.Exp, scale=np.float32(-0.5))
                l9 = pb.do("scalar", fl9, wait=[("l8", l8)], inc=(("l9", 1),))["l9"]

                def fl10(eng):
                    return eng.tensor_scalar(xlt[:], xt[:], lnm[:, 0:1], lnv[:, 0:1],
                                             ALU.subtract, ALU.mult)
                l10 = pb.do("vector", fl10, wait=[("l9", l9)], inc=(("l10", 1),))["l10"]

                qm = emit_fp4q(pb, t, xlt[:], xq[:, 0:D], D, wait=[("l10", l10)])
                for kc in range(NCH):
                    def ftr(eng, kc=kc):
                        return eng.transpose(ps_t2[:], xq[:, kc * 128:(kc + 1) * 128],
                                             t["ident"][:])
                    w = [qm, cst, ("fme", pb.now("fme")), ("hqe", pb.now("hqe"))]
                    if tt == 0 and kc == 0:
                        w.append(wq_done)
                    tr = pb.do("tensor", ftr, wait=w, inc=(("fmt", 1),))["fmt"]

                    def fev(eng, kc=kc, tt=tt):
                        return eng.tensor_copy(
                            xqf[:, kc * S + tt * 128: kc * S + (tt + 1) * 128], ps_t2[:])
                    pb.do("vector", fev, wait=[("fmt", tr)], inc=(("fme", 1),))
            xq_done = ("fme", pb.now("fme"))

            for tt in range(16):
                nof = 0
                for ngi, nsz in enumerate(NGI):
                    gww = [("gmm", pb.now("gmm"))]
                    if b == 0 and tt == 0 and ngi == 0:
                        gww.append(wq_all)
                    lw1 = pb.dma(
                        "gpsimd",
                        wgs[:, 0:NCH * nsz].rearrange("p (kc n) -> p kc n", kc=NCH),
                        wgT[:, nof:nof + nsz].rearrange("(kc p) n -> p kc n", p=128),
                        "gwl", wait=gww)
                    pb.dma(
                        "gpsimd",
                        wus[:, 0:NCH * nsz].rearrange("p (kc n) -> p kc n", kc=NCH),
                        wuT[:, nof:nof + nsz].rearrange("(kc p) n -> p kc n", p=128),
                        "gwl")
                    lwv = pb.now("gwl")

                    def fgm(eng, tt=tt, nsz=nsz):
                        last = None
                        for kc in range(NCH):
                            last = eng.matmul(
                                psq[0][:, 0:nsz],
                                xqf[:, kc * S + tt * 128: kc * S + (tt + 1) * 128],
                                wgs[:, kc * nsz:(kc + 1) * nsz],
                                start=(kc == 0), stop=(kc == NCH - 1))
                        return last

                    def fum(eng, tt=tt, nsz=nsz):
                        last = None
                        for kc in range(NCH):
                            last = eng.matmul(
                                psq[1][:, 0:nsz],
                                xqf[:, kc * S + tt * 128: kc * S + (tt + 1) * 128],
                                wus[:, kc * nsz:(kc + 1) * nsz],
                                start=(kc == 0), stop=(kc == NCH - 1))
                        return last
                    w = [("gwl", lwv), xq_done, ("gsl", pb.now("gsl")),
                         ("dev", pb.now("dev"))]
                    pb.do("tensor", fgm, wait=w, inc=(("gmm", 1),))
                    gm2 = pb.do("tensor", fum, wait=[], inc=(("gmm", 1),))["gmm"]

                    def fsi(eng, nof=nof, nsz=nsz):
                        return eng.activation(ht[:, nof:nof + nsz], psq[0][:, 0:nsz],
                                              AF.Silu)
                    si = pb.do("scalar", fsi, wait=[("gmm", gm2), ("fq", pb.now("fq"))],
                               inc=(("gsi", 1),))["gsi"]

                    def fmu(eng, nof=nof, nsz=nsz):
                        return eng.tensor_mul(ht[:, nof:nof + nsz],
                                              ht[:, nof:nof + nsz], psq[1][:, 0:nsz])
                    pb.do("vector", fmu, wait=[("gsi", si)], inc=(("gsl", 1),))
                    nof += nsz
                h_done = ("gsl", pb.now("gsl"))

                qm = emit_fp4q(pb, t, ht[:], hq[:], I4,
                               wait=[h_done, ("hqe", pb.now("hqe"))])
                for kc in range(22):
                    def ftr(eng, kc=kc):
                        return eng.transpose(ps_t2[:], hq[:, kc * 128:(kc + 1) * 128],
                                             t["ident"][:])
                    w = [qm, cst, ("hqe", pb.now("hqe")), ("fme", pb.now("fme")),
                         ("dmm", pb.now("dmm"))]
                    tr = pb.do("tensor", ftr, wait=w, inc=(("fmt", 1),))["fmt"]

                    def fev(eng, kc=kc):
                        return eng.tensor_copy(hqf[:, kc * 128:(kc + 1) * 128], ps_t2[:])
                    pb.do("vector", fev, wait=[("fmt", tr)], inc=(("hqe", 1),))
                hq_done = ("hqe", pb.now("hqe"))

                lx2 = pb.dma("sync", xt[:], x[b, tt * 128:(tt + 1) * 128, :], "xr2",
                             wait=[("l10", pb.now("l10")), ("dev", pb.now("dev"))])

                for cg in range(2):
                    dww = [("dmm", pb.now("dmm"))]
                    if b == 0 and tt == 0 and cg == 0:
                        dww.append(wq_all)
                    lwd = pb.dma(
                        "gpsimd",
                        wds[cg][:].rearrange("p (kc n) -> p kc n", kc=22),
                        wdT[:, cg * 512:(cg + 1) * 512].rearrange(
                            "(kc p) n -> p kc n", p=128),
                        "dwl", wait=dww)

                    def fdm(eng, cg=cg):
                        last = None
                        for kc in range(22):
                            last = eng.matmul(
                                psq[cg][:],
                                hqf[:, kc * 128:(kc + 1) * 128],
                                wds[cg][:, kc * 512:(kc + 1) * 512],
                                start=(kc == 0), stop=(kc == 21))
                        return last
                    w = [("dwl", pb.now("dwl")), hq_done, ("gsl", pb.now("gsl")),
                         ("dev", pb.now("dev"))]
                    dm = pb.do("tensor", fdm, wait=w, inc=(("dmm", 1),))["dmm"]

                    def fde(eng, cg=cg):
                        return eng.scalar_tensor_tensor(
                            ott[:, cg * 512:(cg + 1) * 512], psq[cg][:], 0.0,
                            xt[:, cg * 512:(cg + 1) * 512], ALU.add, ALU.add)
                    w = [("dmm", dm), ("xr2", lx2),
                         ("ost", 16 * max(0, pb.now("dev") // 2))]
                    pb.do("vector", fde, wait=w, inc=(("dev", 1),))
                ev = pb.now("dev")
                pb.dma("sync", o[b, tt * 128:(tt + 1) * 128, :], ott[:], "ost",
                       wait=[("dev", ev)])

        pb.barrier_wait("sync", [("ost", pb.now("ost"))])
        pb.realize(ctx)
    return nc






def build_attn_mix():
    nc = bass.Bass("TRN2", target_bir_lowering=False)
    x = nc.dram_tensor("x", [BPC, D, S], F32, kind="ExternalInput").ap()
    wq_d = nc.dram_tensor("wq", [D, 4 * D], F32, kind="ExternalInput").ap()
    zout = nc.dram_tensor("z", [BPC, 4, 128, 4096], F32, kind="ExternalOutput").ap()

    pb = PB(nc)
    with ExitStack() as ctx:
        t = {}
        for nm, shp, dt in [
            ("ones_bf", [128, 1], BF16), ("ones_b1", [1, 128], BF16),
            ("mrow", [1, S], BF16), ("rrow", [1, S], BF16),
            ("nsc", [128, 512], F32),
        ]:
            t[nm] = alloc(ctx, nc, nm, shp, dt)
        t["sq"] = [alloc(ctx, nc, f"sq{i}", [128, 512], BF16) for i in range(2)]
        t["psA"] = [alloc_psum(ctx, nc, f"psA{i}", [128, 512]) for i in range(2)]
        xbf = alloc(ctx, nc, "xbf", [128, NCH * S], BF16)
        xln = alloc(ctx, nc, "xln", [128, NCH * S], BF16)
        wt = alloc(ctx, nc, "wt", [128, NCH * D], BF16)
        slab = alloc(ctx, nc, "slab", [128, 16 * D], BF16)
        zsl = alloc(ctx, nc, "zsl", [128, 4096], F32)
        p3 = alloc(ctx, nc, "p3", [128, 4096], F32)
        st_ = alloc(ctx, nc, "st_", [128, 256], F32)
        et_ = alloc(ctx, nc, "et_", [128, 256], F32)
        oac = alloc(ctx, nc, "oac", [128, 256], F32)
        rwt = alloc(ctx, nc, "rwt", [128, 256], F32)
        sm_ = alloc(ctx, nc, "sm_", [128, 16], F32)
        rc_ = alloc(ctx, nc, "rc_", [128, 16], F32)
        xres = alloc(ctx, nc, "xres", [128, 512], F32)
        ot = alloc(ctx, nc, "ot", [128, 512], F32)
        t["vr_view"] = xres[0:1, :]
        t["m2_view"] = ot[0:1, :]

        pb.do("vector", lambda eng: eng.memset(t["ones_bf"][:], 1.0), inc=(("const", 1),))
        pb.do("vector", lambda eng: eng.memset(t["ones_b1"][:], 1.0), inc=(("const", 1),))
        pb.do("gpsimd", lambda eng: eng.nop(), inc=(("const", 1),))

        slab_th = slab[:].tensor

        def view(off, dims):
            return mkAP(slab_th, off, [[16 * D, 128]] + dims)

        for b in range(BPC):
            xw = [("zst", pb.now("zst"))] if b > 0 else []
            for ch in range(NCH):
                pb.dma("gpsimd", xbf[:, ch * S:(ch + 1) * S],
                       x[b, ch * 128:(ch + 1) * 128, :], "xld",
                       wait=xw if ch == 0 else ())
            lnw = ("xld", pb.now("xld"))
            ln_sem, ln_cnt = emit_ln(pb, t, xbf[:], xln[:], lnw)

            for i in range(4):
                ww = [("pmm", pb.now("pmm"))] if (i > 0 or b > 0) else []
                for kc in range(NCH):
                    pb.dma("gpsimd", wt[:, kc * D:(kc + 1) * D],
                           wq_d[kc * 128:(kc + 1) * 128, i * D:(i + 1) * D],
                           "wld", wait=ww if kc == 0 else ())
                wld = pb.now("wld")

                for tt in range(16):
                    for cg in range(2):
                        pi = (tt * 2 + cg) % 2

                        def fmm(eng, tt=tt, cg=cg, pi=pi):
                            last = None
                            for kc in range(NCH):
                                last = eng.matmul(
                                    t["psA"][pi][:],
                                    xln[:, kc * S + tt * 128: kc * S + (tt + 1) * 128],
                                    wt[:, kc * D + cg * 512: kc * D + (cg + 1) * 512],
                                    start=(kc == 0), stop=(kc == NCH - 1))
                            return last
                        w = [("wld", wld), (ln_sem, ln_cnt),
                             ("pev", max(0, pb.now("pmm") - 1))]
                        if tt == 0 and cg == 0:
                            w.append(("zmix", pb.now("zmix")))
                        mm = pb.do("tensor", fmm, wait=w, inc=(("pmm", 1),))["pmm"]

                        def fev(eng, tt=tt, cg=cg, pi=pi):
                            return eng.tensor_copy(
                                slab[:, tt * D + cg * 512: tt * D + (cg + 1) * 512],
                                t["psA"][pi][:])
                        pb.do("vector", fev, wait=[("pmm", mm)], inc=(("pev", 1),))
                slab_done = ("pev", pb.now("pev"))

                for tpos in range(16):
                    sem = "zmix"

                    def chain(fn, w=()):
                        return pb.do("vector", fn,
                                     wait=list(w) + [(sem, pb.now(sem))],
                                     inc=((sem, 1),))[sem]

                    qv = view(tpos * 16, [[1024, 16], [0, 16], [1, 16]])
                    kv = view(256 + tpos * 16, [[0, 16], [1024, 16], [1, 16]])
                    w0 = [slab_done, ("zex", pb.now("zex"))]
                    if tpos == 0:
                        w0.append(("zst", pb.now("zst")))
                    chain(lambda eng, qv=qv, kv=kv: eng.tensor_tensor(
                        p3[:].rearrange("p (a b c) -> p a b c", b=16, c=16),
                        qv, kv, ALU.mult), w=w0)
                    chain(lambda eng: eng.reduce_sum(
                        st_[:].rearrange("p (a b) -> p a b", b=16),
                        p3[:].rearrange("p (a b c) -> p a b c", b=16, c=16),
                        axis=AXX))
                    ex = pb.do("scalar", lambda eng: eng.activation(
                        et_[:], st_[:], AF.Exp, scale=np.float32(0.25)),
                        wait=[(sem, pb.now(sem))],
                        inc=(("zex", 1),))["zex"]
                    chain(lambda eng: eng.reduce_sum(
                        sm_[:], et_[:].rearrange("p (a b) -> p a b", b=16),
                        axis=AXX), w=[("zex", ex)])
                    chain(lambda eng: eng.reciprocal(rc_[:], sm_[:]))
                    evv = mkAP(et_[:].tensor, et_[:].offset,
                               [[256, 128], [16, 16], [0, 16], [1, 16]])
                    vv = view(512 + tpos * 16, [[0, 16], [1, 16], [1024, 16]])
                    chain(lambda eng, evv=evv, vv=vv: eng.tensor_tensor(
                        p3[:].rearrange("p (a b c) -> p a b c", b=16, c=16),
                        evv, vv, ALU.mult))
                    chain(lambda eng: eng.reduce_sum(
                        oac[:].rearrange("p (a b) -> p a b", b=16),
                        p3[:].rearrange("p (a b c) -> p a b c", b=16, c=16),
                        axis=AXX))
                    rcv = mkAP(rc_[:].tensor, rc_[:].offset,
                               [[16, 128], [1, 16], [0, 16]])
                    gv = view(768 + tpos * 16, [[1024, 16], [1, 16]])
                    chain(lambda eng, rcv=rcv, gv=gv: eng.tensor_tensor(
                        rwt[:].rearrange("p (a b) -> p a b", b=16), rcv, gv, ALU.mult))
                    zm = chain(lambda eng, tpos=tpos: eng.tensor_mul(
                        zsl[:, tpos * 256:(tpos + 1) * 256], oac[:], rwt[:]))
                pb.dma("sync", zout[b, i], zsl[:], "zst",
                       wait=[("zmix", pb.now("zmix"))])

        pb.barrier_wait("sync", [("zst", pb.now("zst"))])
        pb.realize(ctx)
    return nc






def _rope_tables():
    HS = 64
    half = HS // 2
    inv_freq = 1.0 / 10000.0 ** (np.arange(half, dtype=np.float32) * 2.0 / HS)
    tpos = np.arange(S, dtype=np.float32)
    r = np.arange(128)
    pair = (r % 64) // 2
    ang = tpos[None, :] * inv_freq[pair][:, None]
    A = np.cos(ang).astype(np.float32)
    Bt = np.sin(ang).astype(np.float32)
    sign = np.where(r % 2 == 0, -1.0, 1.0).astype(np.float32)
    return A, (Bt * sign[:, None]).astype(np.float32)


def _bias_tile(bv):
    return np.ascontiguousarray(bv.reshape(NCH, 128).T)


def _shard_fm(xfull):
    return [np.ascontiguousarray(xfull[2 * c:2 * c + 2].transpose(0, 2, 1))
            for c in range(NCORES)]


def _gather_fm(results, key="o"):
    return np.concatenate([r[key].transpose(0, 2, 1) for r in results], axis=0)


_NC_CACHE = {}


def _get(name, builder, *args):
    if name not in _NC_CACHE:
        _NC_CACHE[name] = builder(*args)
    return _NC_CACHE[name]


def kernel(**inputs):
    inp = inputs
    x = np.ascontiguousarray(inp["L0_attn_a_x"].astype(np.float32))
    cores = list(range(NCORES))
    A, Bt = _rope_tables()
    pair = np.arange(D) ^ 1

    nc = _get("mix", build_attn_mix)
    xs = _shard_fm(x)
    maps = [{"x": xs[c], "wq": inp["L0_attn_a_w_qkvw"]} for c in cores]
    res = run_bass_kernel_spmd(nc, maps, core_ids=cores)
    zouts = [r["z"] for r in res.results]

    zfull = np.empty((B, S, D), dtype=np.float32)
    for c in cores:
        for bl in range(BPC):
            bg = 2 * c + bl
            for i in range(4):
                blk = zouts[c][bl, i].reshape(32, 4, 16, 16, 16)
                blk = blk.transpose(3, 0, 1, 2, 4).reshape(512, 1024)
                bp = i * 4 + bg // 4
                r0 = (bg % 4) * 512
                zfull[bp, r0:r0 + 512, :] = blk

    nc = _get("zproj", build_zproj)
    zs = _shard_fm(zfull)
    maps = [{"x": xs[c], "z": zs[c], "w": inp["L0_attn_a_w_out"]} for c in cores]
    res = run_bass_kernel_spmd(nc, maps, core_ids=cores)
    xcur = [np.ascontiguousarray(r["o"]) for r in res.results]

    moe_w = {
        "wg": inp["L0_ffn_w_gate"],
        "bg": inp["L0_ffn_b_gate"].reshape(1, 8),
        "w1": inp["L0_ffn_W1"],
        "b1": np.ascontiguousarray(np.concatenate(
            [inp["L0_ffn_b1"][e].reshape(32, 128).T for e in range(8)], axis=1)),
        "w2": inp["L0_ffn_W2"],
        "b2": inp["L0_ffn_b2"].reshape(8, 1, D),
    }

    def run_moe(xcur):
        nc = _get("moe", build_moe)
        maps = [dict(moe_w, x=xcur[c]) for c in cores]
        res = run_bass_kernel_spmd(nc, maps, core_ids=cores)
        return [np.ascontiguousarray(r["o"]) for r in res.results]

    xcur = run_moe(xcur)

    nc = _get("attnb", build_attn_std, True, True, True, False)
    am = {
        "wq": inp["L0_attn_b_Wq"], "wk": inp["L0_attn_b_Wk"],
        "wv": inp["L0_attn_b_Wv"], "wp": inp["L0_attn_b_Wp"],
        "wqs": np.ascontiguousarray(inp["L0_attn_b_Wq"][:, pair]),
        "wks": np.ascontiguousarray(inp["L0_attn_b_Wk"][:, pair]),
        "bq": _bias_tile(inp["L0_attn_b_bq"]), "bk": _bias_tile(inp["L0_attn_b_bk"]),
        "bqs": _bias_tile(inp["L0_attn_b_bq"][pair]),
        "bks": _bias_tile(inp["L0_attn_b_bk"][pair]),
        "bv": inp["L0_attn_b_bv"].reshape(1, D),
        "bp": _bias_tile(inp["L0_attn_b_bp"]),
        "ropeA": A, "ropeB": Bt,
    }
    maps = [dict(am, x=xcur[c]) for c in cores]
    res = run_bass_kernel_spmd(nc, maps, core_ids=cores)
    xcur = [np.ascontiguousarray(r["o"]) for r in res.results]

    xcur = run_moe(xcur)

    nc = _get("attna1", build_attn_std, False, False, False, True)
    am = {"wq": inp["L1_attn_a_Wq"], "wk": inp["L1_attn_a_Wk"],
          "wv": inp["L1_attn_a_Wv"], "wp": inp["L1_attn_a_Wp"]}
    maps = [dict(am, x=xcur[c]) for c in cores]
    res = run_bass_kernel_spmd(nc, maps, core_ids=cores)
    xcur = [np.ascontiguousarray(r["o"]) for r in res.results]

    f4_w = {"wg": inp["L1_ffn_w_gate"], "wu": inp["L1_ffn_w_up"],
            "wd": inp["L1_ffn_w_down"]}

    def run_ffn4(xcur_fm):
        nc = _get("ffn4", build_ffn4)
        maps = [dict(f4_w, x=np.ascontiguousarray(xcur_fm[c].transpose(0, 2, 1)))
                for c in cores]
        res = run_bass_kernel_spmd(nc, maps, core_ids=cores)
        return [np.ascontiguousarray(r["o"].transpose(0, 2, 1))
                for r in res.results]

    xcur = run_ffn4(xcur)

    nc = _get("attnrs", build_attn_rs)
    am = {
        "wqkv": inp["L1_attn_b_Wqkv"],
        "bqkv": np.ascontiguousarray(inp["L1_attn_b_bqkv"].reshape(48, 64).T),
        "wp": inp["L1_attn_b_Wproj"],
        "bp": _bias_tile(inp["L1_attn_b_bproj"]),
    }
    maps = [dict(am, x=xcur[c]) for c in cores]
    res = run_bass_kernel_spmd(nc, maps, core_ids=cores)
    xcur = [np.ascontiguousarray(r["o"]) for r in res.results]

    xcur = run_ffn4(xcur)

    out = np.concatenate([xc.transpose(0, 2, 1) for xc in xcur], axis=0)
    return np.ascontiguousarray(out.astype(np.float32))

